# version 09 (final)
# baseline (speedup 1.0000x reference)
import dataclasses
import numpy as np
import concourse.bass as bass
import concourse.mybir as mybir
from concourse.bass_core_rust import IndirectOffsetOnAxis
from concourse.bass_utils import run_bass_kernel_spmd

F32 = mybir.dt.float32
BF16 = mybir.dt.bfloat16
U32 = mybir.dt.uint32
AF = mybir.ActivationFunctionType
ALU = mybir.AluOpType
AX = mybir.AxisListType

DEBUG_TAPS = False
B, T, D = 8, 2048, 1024
NCHUNK = D // 128
NTCH = T // 128
H0, KD0, NK = 8, 128, 256
TOPK = 8


def ap_with(ap, ap_list, extra_offset=0):
    return dataclasses.replace(ap, offset=ap.offset + extra_offset, ap=ap_list)


class Buf:
    __slots__ = ("name", "last_write", "readers")
    def __init__(self, name):
        self.name = name
        self.last_write = None
        self.readers = {}


class Em:
    def __init__(self, nc):
        self.nc = nc
        self.sems = {}
        self.counts = {}
        self.waited = {}
        self.engines = {
            "tensor": nc.tensor, "vector": nc.vector, "scalar": nc.scalar,
            "gpsimd": nc.gpsimd, "sync": nc.sync,
        }
        self._ctxs = []
        for en in list(self.engines):
            self._add_sem(en)
        for en in ("sync", "scalar", "gpsimd", "vector"):
            self._add_sem("dma_" + en)

    def _add_sem(self, key):
        cm = self.nc.semaphore("sem_" + key)
        self.sems[key] = cm.__enter__()
        self._ctxs.append(cm)
        self.counts[key] = 0

    def init_waited(self):
        for en in self.engines:
            self.waited[en] = {}

    def close(self):
        for cm in reversed(self._ctxs):
            cm.__exit__(None, None, None)

    def _wait(self, eng_name, sem_key, val):
        hw = self.waited[eng_name]
        if hw.get(sem_key, 0) >= val:
            return
        self.engines[eng_name].wait_ge(self.sems[sem_key], val)
        hw[sem_key] = val

    def _prep(self, eng_name, reads, writes, raw_deps, sem_key):
        need = {}
        def add(dep):
            if dep is None:
                return
            s, v = dep
            if need.get(s, 0) < v:
                need[s] = v
        for b in reads:
            add(b.last_write)
        for b in writes:
            d = b.last_write
            if d is not None and not (d[0] == sem_key and sem_key.startswith("dma_")):
                add(d)
            for s, v in b.readers.items():
                add((s, v))
        for d in raw_deps:
            add(d)
        for s, v in sorted(need.items()):
            self._wait(eng_name, s, v)

    def _post(self, eng_name, inst, reads, writes, dma):
        sem_key = ("dma_" + eng_name) if dma else eng_name
        inc = 16 if dma else 1
        self.counts[sem_key] += inc
        val = self.counts[sem_key]
        inst.then_inc(self.sems[sem_key], inc)
        for b in writes:
            b.last_write = (sem_key, val)
            b.readers = {}
        for b in reads:
            if b.readers.get(sem_key, 0) < val:
                b.readers[sem_key] = val

    def op(self, eng_name, fn, reads=(), writes=(), dma=False, raw_deps=()):
        sem_key = ("dma_" + eng_name) if dma else eng_name
        self._prep(eng_name, reads, writes, raw_deps, sem_key)
        inst = fn(self.engines[eng_name])
        self._post(eng_name, inst, reads, writes, dma)
        return inst

    def group(self, eng_name, fns, reads=(), writes=(), raw_deps=()):
        self._prep(eng_name, reads, writes, raw_deps, eng_name)
        eng = self.engines[eng_name]
        insts = [fn(eng) for fn in fns]
        self._post(eng_name, insts[-1], reads, writes, False)
        return insts

    def full_barrier(self):
        for en in self.engines:
            for s, v in self.counts.items():
                if v > 0:
                    self._wait(en, s, v)

    def finish(self, eng_name="sync"):
        for s, v in self.counts.items():
            if v > 0:
                self._wait(eng_name, s, v)


def host_prepare(inputs):
    g = {}
    x = np.asarray(inputs["L0_attn_x"])
    g["per_core_xT"] = [np.ascontiguousarray(x[b].T) for b in range(B)]
    g["ones128"] = np.ones((128, 128), dtype=np.float32)
    g["iden"] = np.eye(128, dtype=np.float32)
    mpat = np.zeros((128, 64), dtype=np.float32)
    for r in range(128):
        mpat[r, r % 64] = 1.0 / 128.0
    g["mpat"] = mpat
    g["iota64"] = np.broadcast_to(np.arange(64, dtype=np.float32), (128, 64)).copy()
    g["w_q"] = np.ascontiguousarray(np.asarray(inputs["L0_ffn_w_q"]))
    keys = np.asarray(inputs["L0_ffn_keys"])
    kT = np.zeros((16, 128, 256), dtype=np.float32)
    for p in range(2):
        for h in range(H0):
            kT[p * 8 + h] = keys[h, :, p, :].T
    g["keysT"] = kT
    g["bn_gamma"] = np.ascontiguousarray(np.asarray(inputs["L0_ffn_bn_gamma"]).reshape(16, 128).T)
    g["bn_beta"] = np.ascontiguousarray(np.asarray(inputs["L0_ffn_bn_beta"]).reshape(16, 128).T)
    g["dwu"] = np.ascontiguousarray(np.concatenate(
        [np.asarray(inputs["L0_ffn_down_embed"]), np.asarray(inputs["L0_ffn_up_embed"])], axis=1))
    g["l1_norm_w"] = np.ascontiguousarray(np.asarray(inputs["L1_attn_norm_w"]).reshape(NCHUNK, 128).T)
    g["l1_norm_b"] = np.ascontiguousarray(np.asarray(inputs["L1_attn_norm_b"]).reshape(NCHUNK, 128).T)
    g["qkv_wT"] = np.ascontiguousarray(np.asarray(inputs["L1_attn_qkv_w"]).T)
    qkv_b = np.asarray(inputs["L1_attn_qkv_b"])
    g["bq_pc"] = np.ascontiguousarray(qkv_b[0:1024].reshape(NCHUNK, 128).T)
    g["bkv_row"] = np.ascontiguousarray(qkv_b[1024:3072].reshape(1, 2048))
    g["proj_wT"] = np.ascontiguousarray(np.asarray(inputs["L1_attn_proj_w"]).T)
    g["proj_b_pc"] = np.ascontiguousarray(np.asarray(inputs["L1_attn_proj_b"]).reshape(NCHUNK, 128).T)
    g["router_T"] = np.ascontiguousarray(np.asarray(inputs["L1_ffn_w_router"]).T)
    w_fc = np.asarray(inputs["L1_ffn_w_fc"])
    g["w_fcT"] = np.ascontiguousarray(w_fc.reshape(4096, 1024).T)
    w_proj = np.asarray(inputs["L1_ffn_w_proj"])
    g["w_projT"] = np.ascontiguousarray(w_proj.transpose(0, 2, 1).reshape(4096, 1024))
    p8 = np.zeros((8, 4096), dtype=np.float32)
    for m in range(4096):
        p8[m // 512, m] = 1.0
    g["p8pat"] = p8
    return g


class P:
    def __init__(self, nc, em):
        self.nc = nc
        self.em = em
        self.bufs = {}

    def buf(self, name):
        if name not in self.bufs:
            self.bufs[name] = Buf(name)
        return self.bufs[name]


_LN_SEQ = [0]
def ln_feature_major(p, alloc, x_bufs, x_sb, out_sb, out_bufs, stats, consts, affine=None,
                     double_ln_residual=False):
    import contextlib as _ctx
    nc, em = p.nc, p.em
    _st = _ctx.ExitStack()
    alloc = lambda name, shape, dt=F32: _st.enter_context(nc.sbuf_tensor(name, list(shape), dt))
    ones = consts["ones128_sb"]
    b_ones = p.buf("ones128_sb")
    _LN_SEQ[0] += 1; _i = _LN_SEQ[0]
    sq = alloc(f"ln_sq{_i}", (128, T)); b_sq = p.buf(f"ln_sq{_i}")
    ps_s1 = stats["ps_s1"]; ps_s2 = stats["ps_s2"]
    b_ps1l = stats["b_s1"]; b_ps2l = stats["b_s2"]
    mean = alloc(f"ln_mean{_i}", (128, T)); b_mean = p.buf(f"ln_mean{_i}")
    rstd = alloc(f"ln_rstd{_i}", (128, T)); b_rstd = p.buf(f"ln_rstd{_i}")
    scr = alloc(f"ln_scr{_i}", (128, T)); b_scr = p.buf(f"ln_scr{_i}")
    epst = stats["eps"]; b_eps = p.buf("ln_eps")

    for tg in range(4):
        ts = slice(tg * 512, (tg + 1) * 512)
        em.group("tensor", [
            (lambda k, tg=tg, ts=ts: (lambda t: t.matmul(
                ps_s1[:, ts], ones[:, :], x_sb[:, k, ts],
                start=(k == 0), stop=(k == NCHUNK - 1))))(k)
            for k in range(NCHUNK)
        ], reads=[b_ones] + x_bufs, writes=[b_ps1l[tg]])
    em.op("vector", lambda v: v.tensor_scalar_mul(mean[:, :], ps_s1[:, :], 1.0 / D),
          reads=b_ps1l, writes=[b_mean])
    for k in range(NCHUNK):
        em.op("scalar", lambda s, k=k: s.activation(out=sq[:, :], in_=x_sb[:, k, :], func=AF.Square),
              reads=x_bufs, writes=[b_sq])
        for tg in range(4):
            ts = slice(tg * 512, (tg + 1) * 512)
            em.op("tensor", lambda t, k=k, ts=ts: t.matmul(
                ps_s2[:, ts], ones[:, :], sq[:, ts],
                start=(k == 0), stop=(k == NCHUNK - 1)),
                  reads=[b_ones, b_sq], writes=[b_ps2l[tg]])
    em.op("vector", lambda v: v.tensor_scalar_mul(scr[:, :], ps_s2[:, :], 1.0 / D),
          reads=b_ps2l, writes=[b_scr])
    em.op("vector", lambda v: v.scalar_tensor_tensor(
        rstd[:, :], mean[:, :], -1.0, mean[:, :], ALU.mult, ALU.mult),
          reads=[b_mean], writes=[b_rstd])
    em.op("vector", lambda v: v.tensor_add(rstd[:, :], rstd[:, :], scr[:, :]),
          reads=[b_rstd, b_scr], writes=[b_rstd])
    if double_ln_residual:
        em.op("vector", lambda v: v.tensor_copy(scr[:, :], rstd[:, :]), reads=[b_rstd], writes=[b_scr])
    em.op("scalar", lambda s: s.activation(out=rstd[:, :], in_=rstd[:, :], func=AF.Ln, bias=epst[:, :]),
          reads=[b_rstd, b_eps], writes=[b_rstd])
    em.op("scalar", lambda s: s.activation(out=rstd[:, :], in_=rstd[:, :], func=AF.Exp, scale=-0.5),
          reads=[b_rstd], writes=[b_rstd])
    rcomb = None
    if double_ln_residual:
        em.op("vector", lambda v: v.tensor_mul(scr[:, :], scr[:, :], rstd[:, :]),
              reads=[b_scr, b_rstd], writes=[b_scr])
        em.op("vector", lambda v: v.tensor_mul(scr[:, :], scr[:, :], rstd[:, :]),
              reads=[b_scr, b_rstd], writes=[b_scr])
        em.op("scalar", lambda s: s.activation(out=scr[:, :], in_=scr[:, :], func=AF.Ln, bias=epst[:, :]),
              reads=[b_scr, b_eps], writes=[b_scr])
        em.op("scalar", lambda s: s.activation(out=scr[:, :], in_=scr[:, :], func=AF.Exp, scale=-0.5),
              reads=[b_scr], writes=[b_scr])
        em.op("vector", lambda v: v.tensor_mul(scr[:, :], scr[:, :], rstd[:, :]),
              reads=[b_scr, b_rstd], writes=[b_scr])
        rcomb = scr
    t2 = alloc(f"ln_t2{_i}", (128, T)) if double_ln_residual else None
    b_t2 = p.buf(f"ln_t2{_i}") if double_ln_residual else None
    for k in range(NCHUNK):
        em.op("vector", lambda v, k=k: v.tensor_sub(sq[:, :], x_sb[:, k, :], mean[:, :]),
              reads=x_bufs + [b_mean], writes=[b_sq])
        if double_ln_residual:
            em.op("vector", lambda v, k=k: v.tensor_mul(t2[:, :], sq[:, :], rstd[:, :]),
                  reads=[b_sq, b_rstd], writes=[b_t2])
            em.op("vector", lambda v, k=k: v.tensor_add(x_sb[:, k, :], x_sb[:, k, :], t2[:, :]),
                  reads=[b_t2] + x_bufs, writes=x_bufs)
            w_pc, b_pc, b_wb = affine
            em.op("vector", lambda v, k=k: v.tensor_mul(sq[:, :], sq[:, :], rcomb[:, :]),
                  reads=[b_sq, b_scr], writes=[b_sq])
            em.op("vector", lambda v, k=k: v.tensor_scalar(
                out_sb[:, k, :], sq[:, :], w_pc[:, k:k + 1], b_pc[:, k:k + 1], ALU.mult, ALU.add),
                  reads=[b_sq, b_wb], writes=out_bufs)
        elif affine is None:
            em.op("vector", lambda v, k=k: v.tensor_mul(out_sb[:, k, :], sq[:, :], rstd[:, :]),
                  reads=[b_sq, b_rstd], writes=out_bufs)
        else:
            w_pc, b_pc, b_wb = affine
            em.op("vector", lambda v, k=k: v.tensor_mul(sq[:, :], sq[:, :], rstd[:, :]),
                  reads=[b_sq, b_rstd], writes=[b_sq])
            em.op("vector", lambda v, k=k: v.tensor_scalar(
                out_sb[:, k, :], sq[:, :], w_pc[:, k:k + 1], b_pc[:, k:k + 1], ALU.mult, ALU.add),
                  reads=[b_sq, b_wb], writes=out_bufs)
    em.full_barrier()
    _st.close()


def build_program(debug_taps=DEBUG_TAPS):
    nc = bass.Bass("TRN2", target_bir_lowering=False, debug=False)
    em = Em(nc)
    p = P(nc, em)
    d = {}

    def din(name, shape, dt=F32):
        d[name] = nc.dram_tensor(name, list(shape), dt, kind="ExternalInput").ap()
        return d[name]

    def dout(name, shape, dt=F32):
        d[name] = nc.dram_tensor(name, list(shape), dt, kind="ExternalOutput").ap()
        return d[name]

    din("xT", (D, T))
    din("ones128", (128, 128)); din("iden", (128, 128)); din("mpat", (128, 64))
    din("iota64", (128, 64))
    din("w_q", (D, 2048)); din("keysT", (16, 128, 256))
    din("bn_gamma", (128, 16)); din("bn_beta", (128, 16))
    din("dwu", (65536, 2048))
    din("l1_norm_w", (128, 8)); din("l1_norm_b", (128, 8))
    din("qkv_wT", (D, 3072)); din("bq_pc", (128, 8)); din("bkv_row", (1, 2048))
    din("proj_wT", (D, D)); din("proj_b_pc", (128, 8))
    din("router_T", (D, 8)); din("w_fcT", (D, 4096)); din("w_projT", (4096, D))
    din("p8pat", (8, 4096))
    dout("y", (D, T))
    m_in = nc.dram_tensor("m_bounce_in", [64, T], F32)
    m_out = nc.dram_tensor("m_bounce_out", [64, T], F32)
    bn_in = nc.dram_tensor("bn_bounce_in", [128, 32], F32)
    bn_out = nc.dram_tensor("bn_bounce_out", [128, 32], F32)

    taps = {}
    def tap(name, shape, dt=F32):
        if debug_taps:
            taps[name] = dout("tap_" + name, shape, dt)
        return taps.get(name)

    import contextlib
    stack = contextlib.ExitStack()
    sb = lambda name, shape, dt=F32: stack.enter_context(nc.sbuf_tensor(name, list(shape), dt))
    psum = lambda name, shape, dt=F32: stack.enter_context(nc.psum_tensor(name, list(shape), dt))
    ph_stack = [None]
    def phase_open():
        if ph_stack[0] is not None:
            em.full_barrier()
            ph_stack[0].close()
        ph_stack[0] = contextlib.ExitStack()
    def sbp(name, shape, dt=F32):
        return ph_stack[0].enter_context(nc.sbuf_tensor(name, list(shape), dt))

    with nc.Block() as block:
        em.init_waited()
        x_sb = sb("x_sb", (128, NCHUNK, T))
        lnf = sb("lnf", (128, NCHUNK, T), BF16)
        e_slot = sb("e_slot", (128, NTCH, 64), U32)
        w8slot = sb("w8slot", (128, NTCH, 64), BF16)
        ones128_sb = sb("ones128_sb", (128, 128))
        ones128_bf = sb("ones128_bf", (128, 128), BF16)
        iden_sb = sb("iden_sb", (128, 128))
        mpat_sb = sb("mpat_sb", (128, 64), BF16)
        iota64_sb = sb("iota64_sb", (128, 64))
        consts = {"ones128_sb": ones128_sb, "iden_sb": iden_sb}
        bx = [p.buf("x_sb")]
        blnf = [p.buf("lnf")]

        ps_a = psum("ps_a", (128, T))
        ps_b = psum("ps_b", (128, T))
        bank = lambda i: (ps_a if i < 4 else ps_b)[:, (i % 4) * 512:(i % 4 + 1) * 512]
        bbank = [p.buf(f"bank{i}") for i in range(8)]

        stats = {
            "ps_s1": ps_a,
            "ps_s2": ps_b,
            "eps": sb("ln_eps", (128, 1)),
        }
        stats["b_s1"] = [p.buf(f"bank{i}") for i in range(4)]
        stats["b_s2"] = [p.buf(f"bank{i}") for i in range(4, 8)]

        em.op("sync", lambda s: s.dma_start(out=x_sb[:, :, :], in_=d["xT"].rearrange("(c p) t -> p c t", p=128)),
              writes=bx, dma=True)
        em.op("sync", lambda s: s.dma_start(out=ones128_sb[:, :], in_=d["ones128"][:, :]),
              writes=[p.buf("ones128_sb")], dma=True)
        em.op("sync", lambda s: s.dma_start(out=iden_sb[:, :], in_=d["iden"][:, :]),
              writes=[p.buf("iden_sb")], dma=True)
        em.op("gpsimd", lambda g: g.dma_start(out=mpat_sb[:, :], in_=d["mpat"][:, :]),
              writes=[p.buf("mpat_sb")], dma=True)
        em.op("gpsimd", lambda g: g.dma_start(out=ones128_bf[:, :], in_=d["ones128"][:, :]),
              writes=[p.buf("ones128_bf")], dma=True)
        em.op("sync", lambda s: s.dma_start(out=iota64_sb[:, :], in_=d["iota64"][:, :]),
              writes=[p.buf("iota64_sb")], dma=True)
        em.op("vector", lambda v: v.memset(stats["eps"][:, :], 1e-5), writes=[p.buf("ln_eps")])

        phase_open()
        ln_feature_major(p, sbp, bx, x_sb, lnf, blnf, stats, consts)
        if debug_taps:
            em.op("scalar", lambda s: s.dma_start(out=taps["ln1"][:, :, :], in_=lnf[:, :, :]),
                  reads=blnf, dma=True) if tap("ln1", (128, NCHUNK, T), BF16) is not None else None

        phase_open()
        lnf_bufs = blnf
        m_sb = sbp("m_sb", (64, T))
        b_msb = p.buf("m_sb")
        for tg in range(4):
            ts = slice(tg * 512, (tg + 1) * 512)
            em.group("tensor", [
                (lambda k, tg=tg, ts=ts: (lambda t: t.matmul(
                    ps_a[0:64, ts], mpat_sb[:, :], lnf[:, k, ts],
                    start=(k == 0), stop=(k == NCHUNK - 1))))(k)
                for k in range(NCHUNK)
            ], reads=[p.buf("mpat_sb")] + lnf_bufs, writes=[bbank[tg]])
        em.op("vector", lambda v: v.tensor_copy(m_sb[:, :], ps_a[0:64, :]), reads=bbank[0:4], writes=[b_msb])
        b_mbounce = p.buf("m_bounce")
        em.op("gpsimd", lambda g: g.dma_start(out=m_in.ap(), in_=m_sb[:, :]), reads=[b_msb],
              writes=[b_mbounce], dma=True)
        em.op("gpsimd", lambda g: g.collective_compute(
            "AllReduce", ALU.add, replica_groups=[list(range(B))],
            ins=[m_in.ap()], outs=[m_out.ap()]), reads=[b_mbounce], writes=[b_mbounce])
        em.op("gpsimd", lambda g: g.dma_start(out=m_sb[:, :], in_=m_out.ap()),
              reads=[b_mbounce], writes=[b_msb], dma=True)
        m_bf = sbp("m_bf", (64, T), BF16)
        b_mbf = p.buf("m_bf")
        em.op("vector", lambda v: v.tensor_copy(m_bf[:, :], m_sb[:, :]), reads=[b_msb], writes=[b_mbf])
        if debug_taps and tap("m", (64, T)) is not None:
            em.op("scalar", lambda s: s.dma_start(out=taps["m"][:, :], in_=m_sb[:, :]),
                  reads=[b_msb], dma=True)

        ln1_tm = sbp("ln1_tm", (128, NTCH, D), BF16)
        b_tm = p.buf("ln1_tm")
        for k in range(NCHUNK):
            em.op("sync", lambda s, k=k: s.dma_start(
                out=ap_with(ln1_tm[:, :, :], [[NTCH * D, 128], [D, NTCH], [1, 128]], extra_offset=k * 128),
                in_=lnf[:, k, :], transpose=True),
                  reads=lnf_bufs, writes=[b_tm], dma=True)

        e_ig = sbp("e_ig", (128, NTCH, 512), BF16)
        b_eig = p.buf("e_ig")
        ang_ps = bank(0)
        b_angps = bbank[0]
        srep_ps = bank(1)
        b_srps = bbank[1]
        invs = sbp("invs", (128, 512))
        b_invs = p.buf("invs")
        escr = sbp("escr", (128, 512))
        b_escr = p.buf("escr")
        ap_ps = [bank(2), bank(3)]
        b_apps = [bbank[2], bbank[3]]

        for ig in range(4):
            igs = slice(ig * 512, (ig + 1) * 512)
            for jc in range(NTCH):
                em.op("tensor", lambda t, jc=jc, igs=igs: t.matmul(
                    ang_ps[:, :], m_bf[:, jc * 128:(jc + 1) * 128], m_bf[:, igs],
                    start=True, stop=True), reads=[b_mbf], writes=[b_angps])
                em.op("vector", lambda v: v.tensor_scalar(
                    escr[:, :], ang_ps[:, :], float(np.pi), float(-np.pi), ALU.min, ALU.max),
                      reads=[b_angps], writes=[b_escr])
                em.op("scalar", lambda s: s.activation(out=escr[:, :], in_=escr[:, :], func=AF.Sin, scale=0.5),
                      reads=[b_escr], writes=[b_escr])
                em.op("scalar", lambda s: s.activation(out=escr[:, :], in_=escr[:, :], func=AF.Square),
                      reads=[b_escr], writes=[b_escr])
                em.op("scalar", lambda s, jc=jc: s.activation(out=e_ig[:, jc, :], in_=escr[:, :], func=AF.Exp),
                      reads=[b_escr], writes=[b_eig])
            em.group("tensor", [
                (lambda jc: (lambda t: t.matmul(
                    srep_ps[:, :], ones128_bf[:, :], e_ig[:, jc, :],
                    start=(jc == 0), stop=(jc == NTCH - 1))))(jc)
                for jc in range(NTCH)
            ], reads=[p.buf("ones128_bf"), b_eig], writes=[b_srps])
            em.op("scalar", lambda s: s.activation(out=invs[:, :], in_=srep_ps[:, :], func=AF.Ln),
                  reads=[b_srps], writes=[b_invs])
            em.op("scalar", lambda s: s.activation(out=invs[:, :], in_=invs[:, :], func=AF.Exp, scale=-1.0),
                  reads=[b_invs], writes=[b_invs])
            for k in range(NCHUNK):
                pp = ap_ps[k % 2]
                bpp = b_apps[k % 2]
                em.group("tensor", [
                    (lambda jc, k=k: (lambda t: t.matmul(
                        pp[:, :], ln1_tm[:, jc, k * 128:(k + 1) * 128], e_ig[:, jc, :],
                        start=(jc == 0), stop=(jc == NTCH - 1))))(jc)
                    for jc in range(NTCH)
                ], reads=[b_tm, b_eig], writes=[bpp])
                em.op("vector", lambda v, k=k: v.tensor_mul(escr[:, :], pp[:, :], invs[:, :]),
                      reads=[bpp, b_invs], writes=[b_escr])
                em.op("vector", lambda v, k=k, igs=igs: v.tensor_add(
                    x_sb[:, k, igs], x_sb[:, k, igs], escr[:, :]),
                      reads=[b_escr] + bx, writes=bx)

        if debug_taps and tap("x1", (128, NCHUNK, T)) is not None:
            em.op("scalar", lambda s: s.dma_start(out=taps["x1"][:, :, :], in_=x_sb[:, :, :]),
                  reads=bx, dma=True)


        phase_open()
        ln_feature_major(p, sbp, bx, x_sb, lnf, blnf, stats, consts)

        phase_open()
        q_bn = sbp("q_bn", (128, 16, T), BF16)
        b_qbn = p.buf("q_bn")
        qs1 = sbp("qs1", (128, 16))
        qs2 = sbp("qs2", (128, 16))
        b_qs = p.buf("qs")
        wq_stack = contextlib.ExitStack()
        sqs = wq_stack.enter_context(nc.sbuf_tensor("qbn_scr", [128, T], F32)); b_sqs = p.buf("qbn_scr")
        wq_sb = wq_stack.enter_context(nc.sbuf_tensor("wq_sb", [128, NCHUNK, 1024], BF16))
        b_wq = p.buf("wq_sb")
        for of in range(16):
            if of % 8 == 0:
                em.op("gpsimd", lambda g, of=of: g.dma_start(
                    out=wq_sb[:, :, :],
                    in_=d["w_q"].rearrange("(c p) o -> p c o", p=128)[:, :, of * 128:(of + 8) * 128]),
                      writes=[b_wq], dma=True)
            for tg in range(4):
                ts = slice(tg * 512, (tg + 1) * 512)
                bk = (of * 4 + tg) % 8
                em.group("tensor", [
                    (lambda ck, of=of, ts=ts, bk=bk: (lambda t: t.matmul(
                        bank(bk), wq_sb[:, ck, (of % 8) * 128:(of % 8 + 1) * 128], lnf[:, ck, ts],
                        start=(ck == 0), stop=(ck == NCHUNK - 1))))(ck)
                    for ck in range(NCHUNK)
                ], reads=[b_wq] + blnf, writes=[bbank[bk]])
                em.op("vector", lambda v, of=of, ts=ts, bk=bk: v.tensor_copy(
                    q_bn[:, of, ts], bank(bk)), reads=[bbank[bk]], writes=[b_qbn])
            em.op("vector", lambda v, of=of: v.reduce_sum(
                qs1[:, of:of + 1], q_bn[:, of, :], AX.X), reads=[b_qbn], writes=[b_qs])
            em.op("scalar", lambda s, of=of: s.activation(
                out=sqs[:, :], in_=q_bn[:, of, :], func=AF.Square,
                accum_out=qs2[:, of:of + 1]), reads=[b_qbn], writes=[b_sqs, b_qs])
        em.full_barrier()
        wq_stack.close()
        b_bnb = p.buf("bn_bounce")
        em.op("gpsimd", lambda g: g.dma_start(out=bn_in.ap()[:, 0:16], in_=qs1[:, :]),
              reads=[b_qs], writes=[b_bnb], dma=True)
        em.op("gpsimd", lambda g: g.dma_start(out=bn_in.ap()[:, 16:32], in_=qs2[:, :]),
              reads=[b_qs], writes=[b_bnb], dma=True)
        em.op("gpsimd", lambda g: g.collective_compute(
            "AllReduce", ALU.add, replica_groups=[list(range(B))],
            ins=[bn_in.ap()], outs=[bn_out.ap()]), reads=[b_bnb], writes=[b_bnb])
        bns = sbp("bns", (128, 32))
        b_bns = p.buf("bns")
        em.op("gpsimd", lambda g: g.dma_start(out=bns[:, :], in_=bn_out.ap()),
              reads=[b_bnb], writes=[b_bns], dma=True)
        gam = sbp("gam", (128, 16)); bet = sbp("bet", (128, 16))
        b_gb = p.buf("gam_bet")
        em.op("sync", lambda s: s.dma_start(out=gam[:, :], in_=d["bn_gamma"][:, :]), writes=[b_gb], dma=True)
        em.op("sync", lambda s: s.dma_start(out=bet[:, :], in_=d["bn_beta"][:, :]), writes=[b_gb], dma=True)
        bnm = sbp("bnm", (128, 16)); bnv = sbp("bnv", (128, 16))
        b_bnm = p.buf("bnm"); b_bnv = p.buf("bnv")
        NT = float(B * T)
        em.op("vector", lambda v: v.tensor_scalar_mul(bnm[:, :], bns[:, 0:16], 1.0 / NT),
              reads=[b_bns], writes=[b_bnm])
        em.op("vector", lambda v: v.tensor_scalar_mul(bnv[:, :], bns[:, 16:32], 1.0 / NT),
              reads=[b_bns], writes=[b_bnv])
        bnscr = sbp("bnscr", (128, 16)); b_bnscr = p.buf("bnscr")
        em.op("vector", lambda v: v.scalar_tensor_tensor(
            bnscr[:, :], bnm[:, :], -1.0, bnm[:, :], ALU.mult, ALU.mult),
              reads=[b_bnm], writes=[b_bnscr])
        em.op("vector", lambda v: v.tensor_add(bnv[:, :], bnv[:, :], bnscr[:, :]),
              reads=[b_bnv, b_bnscr], writes=[b_bnv])
        em.op("scalar", lambda s: s.activation(out=bnv[:, :], in_=bnv[:, :], func=AF.Ln, bias=stats["eps"][:, :]),
              reads=[b_bnv, p.buf("ln_eps")], writes=[b_bnv])
        em.op("scalar", lambda s: s.activation(out=bnv[:, :], in_=bnv[:, :], func=AF.Exp, scale=-0.5),
              reads=[b_bnv], writes=[b_bnv])
        em.op("vector", lambda v: v.tensor_mul(gam[:, :], gam[:, :], bnv[:, :]),
              reads=[b_gb, b_bnv], writes=[b_gb])
        em.op("vector", lambda v: v.scalar_tensor_tensor(
            bnm[:, :], bnm[:, :], -1.0, gam[:, :], ALU.mult, ALU.mult),
              reads=[b_bnm, b_gb], writes=[b_bnm])
        em.op("vector", lambda v: v.tensor_add(bet[:, :], bet[:, :], bnm[:, :]),
              reads=[b_gb, b_bnm], writes=[b_gb])
        for of in range(16):
            em.op("vector", lambda v, of=of: v.tensor_scalar(
                q_bn[:, of, :], q_bn[:, of, :], gam[:, of:of + 1], bet[:, of:of + 1],
                ALU.mult, ALU.add), reads=[b_qbn, b_gb], writes=[b_qbn])
        if debug_taps and tap("qbn", (128, 16, T), BF16) is not None:
            em.op("scalar", lambda s: s.dma_start(out=taps["qbn"][:, :, :], in_=q_bn[:, :, :]),
                  reads=[b_qbn], dma=True)

        keysT_sb = sbp("keysT_sb", (128, 16, 256), BF16)
        b_keys = p.buf("keysT_sb")
        em.op("gpsimd", lambda g: g.dma_start(
            out=keysT_sb[:, :, :], in_=d["keysT"].rearrange("a p k -> p a k")),
              writes=[b_keys], dma=True)
        sim_sb = sbp("sim_sb", (128, 16, 256))
        b_sim = p.buf("sim_sb")
        m8t = sbp("m8t", (128, 128)); b_m8 = p.buf("m8t")
        i8t = sbp("i8t", (128, 128), U32); b_i8 = p.buf("i8t")
        i8f = sbp("i8f", (128, 128)); b_i8f = p.buf("i8f")
        comb_s = sbp("comb_s", (128, 512)); b_cs = p.buf("comb_s")
        comb_e = sbp("comb_e", (128, 512)); b_ce = p.buf("comb_e")
        s8 = sbp("s8", (128, 64)); b_s8 = p.buf("s8")
        pk = sbp("pk", (128, 64), U32); b_pk = p.buf("pk")
        pkf = sbp("pkf", (128, 64)); b_pkf = p.buf("pkf")
        rt_scr = dataclasses.replace(sim_sb[:, :, :], ap=[[4096, 128], [1, 4096]]); b_rts = p.buf("sim_sb")
        ssum = sbp("ssum", (128, 8)); b_ssum = p.buf("ssum")
        b_esl_t = [p.buf(f"e_slot{i}") for i in range(NTCH)]
        b_w8s_t = [p.buf(f"w8slot{i}") for i in range(NTCH)]
        tp_ps = bank(0)
        b_tpps = bbank[0]

        for it in range(NTCH):
            tsl = slice(it * 128, (it + 1) * 128)
            for ph in range(16):
                bk = ph % 8
                cs = slice((ph // 8) * 256, (ph // 8) * 256 + 256)
                em.op("tensor", lambda t, ph=ph, bk=bk, cs=cs, tsl=tsl: t.matmul(
                    (ps_a if bk < 4 else ps_b)[:, (bk % 4) * 512 + (ph // 8) * 256:
                                               (bk % 4) * 512 + (ph // 8) * 256 + 256],
                    q_bn[:, ph, tsl], keysT_sb[:, ph, :], start=True, stop=True),
                      reads=[b_qbn, b_keys], writes=[bbank[bk]])
            for ph in range(16):
                bk = ph % 8
                em.op("vector", lambda v, ph=ph, bk=bk: v.tensor_copy(
                    sim_sb[:, ph, :],
                    (ps_a if bk < 4 else ps_b)[:, (bk % 4) * 512 + (ph // 8) * 256:
                                               (bk % 4) * 512 + (ph // 8) * 256 + 256]),
                      reads=[bbank[bk]], writes=[b_sim])
            for ph in range(16):
                em.op("vector", lambda v, ph=ph: v.max(
                    out=m8t[:, ph * 8:(ph + 1) * 8], in_=sim_sb[:, ph, :]),
                      reads=[b_sim], writes=[b_m8])
                em.op("vector", lambda v, ph=ph: v.max_index(
                    out=i8t[:, ph * 8:(ph + 1) * 8], in_max=m8t[:, ph * 8:(ph + 1) * 8],
                    in_values=sim_sb[:, ph, :]),
                      reads=[b_sim, b_m8], writes=[b_i8])
            em.op("vector", lambda v: v.tensor_copy(i8f[:, :], i8t[:, :]),
                  reads=[b_i8], writes=[b_i8f])
            sx_ap = ap_with(m8t[:, :], [[128, 128], [8, 8], [1, 8], [0, 8]])
            sy_ap = ap_with(m8t[:, :], [[128, 128], [8, 8], [0, 8], [1, 8]], extra_offset=64)
            em.op("vector", lambda v: v.tensor_tensor(
                ap_with(comb_s[:, :], [[512, 128], [64, 8], [8, 8], [1, 8]]),
                sx_ap, sy_ap, ALU.add), reads=[b_m8], writes=[b_cs])
            em.op("vector", lambda v: v.tensor_scalar_mul(i8f[:, 0:64], i8f[:, 0:64], 256.0),
                  reads=[b_i8f], writes=[b_i8f])
            ix_ap = ap_with(i8f[:, :], [[128, 128], [8, 8], [1, 8], [0, 8]])
            iy_ap = ap_with(i8f[:, :], [[128, 128], [8, 8], [0, 8], [1, 8]], extra_offset=64)
            em.op("vector", lambda v: v.tensor_tensor(
                ap_with(comb_e[:, :], [[512, 128], [64, 8], [8, 8], [1, 8]]),
                ix_ap, iy_ap, ALU.add), reads=[b_i8f], writes=[b_ce])
            for h in range(8):
                em.op("vector", lambda v, h=h: v.max(
                    out=s8[:, h * 8:(h + 1) * 8], in_=comb_s[:, h * 64:(h + 1) * 64]),
                      reads=[b_cs], writes=[b_s8])
                em.op("vector", lambda v, h=h: v.max_index(
                    out=pk[:, h * 8:(h + 1) * 8], in_max=s8[:, h * 8:(h + 1) * 8],
                    in_values=comb_s[:, h * 64:(h + 1) * 64]),
                      reads=[b_cs, b_s8], writes=[b_pk])
            em.op("vector", lambda v: v.tensor_copy(pkf[:, :], pk[:, :]),
                  reads=[b_pk], writes=[b_pkf])
            pk_b = ap_with(pkf[:, :], [[64, 128], [1, 64], [0, 64]])
            io_b = ap_with(iota64_sb[:, :], [[64, 128], [0, 64], [1, 64]])
            em.op("vector", lambda v: v.tensor_tensor(
                ap_with(rt_scr, [[4096, 128], [64, 64], [1, 64]]),
                pk_b, io_b, ALU.is_equal), reads=[b_pkf, p.buf("iota64_sb")], writes=[b_rts])
            ce_b = ap_with(comb_e[:, :], [[512, 128], [64, 8], [0, 8], [1, 64]])
            em.op("vector", lambda v: v.tensor_tensor(
                ap_with(rt_scr, [[4096, 128], [64, 64], [1, 64]]),
                ap_with(rt_scr, [[4096, 128], [64, 64], [1, 64]]), ce_b, ALU.mult),
                  reads=[b_rts, b_ce], writes=[b_rts])
            em.op("vector", lambda v: v.tensor_reduce(
                pkf[:, :], ap_with(rt_scr, [[4096, 128], [64, 64], [1, 64]]),
                AX.X, ALU.add), reads=[b_rts], writes=[b_pkf])
            smax_ap = ap_with(s8[:, :], [[64, 128], [8, 8], [0, 8]])
            em.op("vector", lambda v: v.tensor_sub(
                ap_with(comb_s[:, 0:64], [[512, 128], [8, 8], [1, 8]]),
                ap_with(s8[:, :], [[64, 128], [8, 8], [1, 8]]), smax_ap),
                  reads=[b_s8], writes=[b_cs])
            em.op("scalar", lambda s: s.activation(out=comb_s[:, 0:64], in_=comb_s[:, 0:64], func=AF.Exp),
                  reads=[b_cs], writes=[b_cs])
            em.op("vector", lambda v: v.tensor_reduce(
                ssum[:, :], ap_with(comb_s[:, 0:64], [[512, 128], [8, 8], [1, 8]]),
                AX.X, ALU.add), reads=[b_cs], writes=[b_ssum])
            em.op("vector", lambda v: v.reciprocal(ssum[:, :], ssum[:, :]),
                  reads=[b_ssum], writes=[b_ssum])
            em.op("vector", lambda v: v.tensor_mul(
                ap_with(s8[:, :], [[64, 128], [8, 8], [1, 8]]),
                ap_with(comb_s[:, 0:64], [[512, 128], [8, 8], [1, 8]]),
                ap_with(ssum[:, :], [[8, 128], [1, 8], [0, 8]])),
                  reads=[b_cs, b_ssum], writes=[b_s8])
            em.op("tensor", lambda t: t.transpose(tp_ps[0:64, 0:128], pkf[:, 0:64], iden_sb[:, :]),
                  reads=[b_pkf, p.buf("iden_sb")], writes=[b_tpps])
            em.op("vector", lambda v, it=it: v.tensor_copy(
                e_slot[0:64, it, :], ap_with(tp_ps[0:64, 0:128], [[2048, 64], [2, 64]])),
                  reads=[b_tpps], writes=[b_esl_t[it]])
            em.op("vector", lambda v, it=it: v.tensor_copy(
                e_slot[64:128, it, :], ap_with(tp_ps[0:64, 0:128], [[2048, 64], [2, 64]], extra_offset=1)),
                  reads=[b_tpps], writes=[b_esl_t[it]])
            em.op("tensor", lambda t: t.transpose(tp_ps[0:64, 0:128], s8[:, 0:64], iden_sb[:, :]),
                  reads=[b_s8, p.buf("iden_sb")], writes=[b_tpps])
            em.op("vector", lambda v, it=it: v.tensor_copy(
                w8slot[0:64, it, :], ap_with(tp_ps[0:64, 0:128], [[2048, 64], [2, 64]])),
                  reads=[b_tpps], writes=[b_w8s_t[it]])
            em.op("vector", lambda v, it=it: v.tensor_copy(
                w8slot[64:128, it, :], ap_with(tp_ps[0:64, 0:128], [[2048, 64], [2, 64]], extra_offset=1)),
                  reads=[b_tpps], writes=[b_w8s_t[it]])
        if debug_taps:
            tap("eslot", (128, NTCH, 64), U32)
            tap("w8slot", (128, NTCH, 64), BF16)
            em.op("scalar", lambda s: s.dma_start(out=taps["eslot"][:, :, :], in_=e_slot[:, :, :]),
                  reads=b_esl_t, dma=True)
            em.op("scalar", lambda s: s.dma_start(out=taps["w8slot"][:, :, :], in_=w8slot[:, :, :]),
                  reads=b_w8s_t, dma=True)


        phase_open()
        NBUF = 3
        gbuf = [sbp(f"gbuf{i}", (128, 2, 2048)) for i in range(NBUF)]
        b_gb = [p.buf(f"gbuf{i}") for i in range(NBUF)]
        gbf = [sbp(f"gbf{i}", (128, 2, 2048), BF16) for i in range(2)]
        b_gbf = [p.buf(f"gbf{i}") for i in range(2)]
        wdT_sb = [sbp(f"wdT_sb{i}", (128, NCHUNK, 512), BF16) for i in range(2)]
        b_wdT = [p.buf(f"wdT_sb{i}") for i in range(2)]
        G_sb = sbp("G_sb", (32, 2048))
        b_G = p.buf("G_sb")
        hg = sbp("hg", (128, 16)); b_hg = p.buf("hg")
        Pc = [sbp("Pc0", (128, 512), BF16), sbp("Pc1", (128, 512), BF16)]
        b_Pc = [p.buf("Pc0"), p.buf("Pc1")]
        usmall = sbp("usmall", (32, 1024))
        b_us = p.buf("usmall")

        dwu_win = d["dwu"][0:128, :]
        b_dots = [bbank[0], bbank[1], bbank[2], bbank[3]]
        b_gt = bbank[6]
        b_up = [bbank[4], bbank[5]]
        b_tt = bbank[7]

        NM = 8
        for g in range(64):
            it = g // 4
            c0 = (g % 4) * 16
            jrow = g % 4
            tok0 = g * 32
            pc = Pc[g % 2]; bpc = b_Pc[g % 2]
            for m in range(NM):
                bi = m % NBUF
                gb = gbuf[bi]; bgb = b_gb[bi]
                gf = gbf[m % 2]; bgf = b_gbf[m % 2]
                wt = wdT_sb[(m // 2) % 2]; bwt = b_wdT[(m // 2) % 2]
                for c in range(2):
                    col = c0 + m * 2 + c
                    em.op("gpsimd", lambda gp, gb=gb, c=c, it=it, col=col: gp.indirect_dma_start(
                        gb[:, c, :], None, dwu_win,
                        IndirectOffsetOnAxis(ap=e_slot[:, it, col:col + 1], axis=0)),
                          reads=[b_esl_t[it]], writes=[bgb], dma=True)
                    em.op("scalar", lambda s, gb=gb, gf=gf, c=c: s.activation(
                        out=gf[:, c, :], in_=gb[:, c, :], func=AF.Copy),
                          reads=[bgb], writes=[bgf])
                    em.op("sync", lambda s, c=c, m=m, gf=gf, wt=wt: s.dma_start(
                        out=ap_with(wt[:, :, :], [[4096, 128], [512, 8], [1, 128]],
                                    extra_offset=(m % 2) * 256 + c * 128),
                        in_=gf[:, c, 0:1024], transpose=True),
                          reads=[bgf], writes=[bwt], dma=True)
                if m % 2 == 1:
                    bk = m // 2
                    em.group("tensor", [
                        (lambda dc, bk=bk, wt=wt, tok0=tok0: (lambda t: t.matmul(
                            ps_a[0:32, bk * 512:(bk + 1) * 512],
                            lnf[:, dc, tok0:tok0 + 32], wt[:, dc, :],
                            start=(dc == 0), stop=(dc == NCHUNK - 1))))(dc)
                        for dc in range(NCHUNK)
                    ], reads=blnf + [bwt], writes=[b_dots[bk]])
            em.op("scalar", lambda s: s.activation(
                out=G_sb[0:32, :], in_=ps_a[0:32, :], func=AF.Gelu_apprx_tanh),
                  reads=b_dots, writes=[b_G])
            em.group("tensor", [
                (lambda k: (lambda t: t.transpose(
                    bank(6)[:, k * 32:(k + 1) * 32], G_sb[0:32, k * 128:(k + 1) * 128],
                    iden_sb[0:32, 0:32])))(k)
                for k in range(16)
            ], reads=[b_G, p.buf("iden_sb")], writes=[b_gt])
            em.op("vector", lambda v: v.tensor_copy(
                hg[0:64, :], ap_with(bank(6)[0:64, 0:512], [[2048, 64], [34, 16]])),
                  reads=[b_gt], writes=[b_hg])
            em.op("vector", lambda v: v.tensor_copy(
                hg[64:128, :], ap_with(bank(6)[0:64, 0:512], [[2048, 64], [34, 16]], extra_offset=1)),
                  reads=[b_gt], writes=[b_hg])
            em.op("vector", lambda v, it=it, c0=c0: v.tensor_mul(
                hg[:, :], hg[:, :], w8slot[:, it, c0:c0 + 16]),
                  reads=[b_hg, b_w8s_t[it]], writes=[b_hg])
            em.op("vector", lambda v, pc=pc: v.memset(pc[:, :], 0.0), writes=[bpc])
            em.op("vector", lambda v, pc=pc: v.tensor_copy(
                ap_with(pc[0:64, :], [[512, 64], [34, 16]]), hg[0:64, :]),
                  reads=[b_hg], writes=[bpc])
            em.op("vector", lambda v, pc=pc: v.tensor_copy(
                ap_with(pc[0:64, :], [[512, 64], [34, 16]], extra_offset=1), hg[64:128, :]),
                  reads=[b_hg], writes=[bpc])
            for m in range(NM):
                gf = gbf[m % 2]; bgf = b_gbf[m % 2]
                for c in range(2):
                    kk = m * 2 + c
                    for dh in range(2):
                        em.op("tensor", lambda t, kk=kk, c=c, dh=dh, gf=gf, jrow=jrow, pc=pc: t.matmul(
                            bank(4 + dh)[32 * jrow:32 * (jrow + 1), :],
                            pc[:, kk * 32:(kk + 1) * 32], gf[:, c, 1024 + dh * 512:1024 + dh * 512 + 512],
                            start=(kk == 0), stop=(kk == 15),
                            tile_position=(0, 32 * jrow)),
                              reads=[bpc, bgf], writes=[b_up[dh]])
            for dh in range(2):
                em.op("vector", lambda v, dh=dh, jrow=jrow: v.tensor_copy(
                    usmall[0:32, dh * 512:(dh + 1) * 512],
                    bank(4 + dh)[32 * jrow:32 * (jrow + 1), :]),
                      reads=[b_up[dh]], writes=[b_us])
            for half in range(2):
                em.group("tensor", [
                    (lambda q, half=half: (lambda t: t.transpose(
                        bank(7)[:, q * 32 + half * 128:q * 32 + half * 128 + 32],
                        usmall[0:32, half * 512 + q * 128:half * 512 + q * 128 + 128],
                        iden_sb[0:32, 0:32])))(q)
                    for q in range(4)
                ], reads=[b_us, p.buf("iden_sb")], writes=[b_tt])
            for dc in range(NCHUNK):
                half, q = dc // 4, dc % 4
                em.op("vector", lambda v, dc=dc, half=half, q=q, tok0=tok0: v.tensor_add(
                    x_sb[:, dc, tok0:tok0 + 32], x_sb[:, dc, tok0:tok0 + 32],
                    bank(7)[:, q * 32 + half * 128:q * 32 + half * 128 + 32]),
                      reads=[b_tt] + bx, writes=bx)
        phase_open()
        lw = sbp("lw", (128, 8)); lb = sbp("lb", (128, 8)); b_lwb = p.buf("lwb")
        em.op("sync", lambda s: s.dma_start(out=lw[:, :], in_=d["l1_norm_w"][:, :]), writes=[b_lwb], dma=True)
        em.op("sync", lambda s: s.dma_start(out=lb[:, :], in_=d["l1_norm_b"][:, :]), writes=[b_lwb], dma=True)
        ln_feature_major(p, sbp, bx, x_sb, lnf, blnf, stats, consts, affine=(lw, lb, b_lwb),
                         double_ln_residual=True)

        KV_sb = sbp("KV_sb", (128, NCHUNK, D), BF16)
        b_KV = p.buf("KV_sb")
        rows_sb = sbp("rows_sb", (1, 4096), BF16)
        b_rows = p.buf("rows_sb")
        rowf = sbp("rowf", (1, 1024))
        b_rowf = p.buf("rowf")
        bqf = sbp("bqf", (128, 8))
        b_bqf = p.buf("bqf")
        em.op("sync", lambda s: s.dma_start(out=bqf[:, :], in_=d["bq_pc"][:, :]),
              writes=[b_bqf], dma=True)
        em.op("gpsimd", lambda g: g.dma_start(out=rows_sb[0:1, 0:2048], in_=d["bkv_row"][:, :]),
              writes=[b_rows], dma=True)


        kv_stack = contextlib.ExitStack()
        k_tm = kv_stack.enter_context(nc.sbuf_tensor("k_tm", [128, NTCH, D], BF16))
        v_tm = kv_stack.enter_context(nc.sbuf_tensor("v_tm", [128, NTCH, D], BF16))
        wt_sb = kv_stack.enter_context(nc.sbuf_tensor("wt_sb", [128, NCHUNK, 512], BF16))
        b_ktm = p.buf("k_tm"); b_vtm = p.buf("v_tm"); b_wt = p.buf("wt_sb")
        for which, dst, bdst in (("k", k_tm, b_ktm), ("v", v_tm, b_vtm)):
            off0 = 1024 if which == "k" else 2048
            for og in range(2):
                em.op("gpsimd", lambda g, off=off0 + og * 512: g.dma_start(
                    out=wt_sb[:, :, :],
                    in_=d["qkv_wT"].rearrange("(c p) o -> p c o", p=128)[:, :, off:off + 512]),
                      writes=[b_wt], dma=True)
                for tc in range(NTCH):
                    bk = (tc * 2 + og) % 8
                    em.group("tensor", [
                        (lambda ck, tc=tc, bk=bk: (lambda t: t.matmul(
                            bank(bk), lnf[:, ck, tc * 128:(tc + 1) * 128],
                            wt_sb[:, ck, :],
                            start=(ck == 0), stop=(ck == NCHUNK - 1))))(ck)
                        for ck in range(NCHUNK)
                    ], reads=blnf + [b_wt], writes=[bbank[bk]])
                    em.op("vector", lambda v, tc=tc, og=og, bk=bk, dst=dst, bdst=bdst: v.tensor_copy(
                        dst[:, tc, og * 512:(og + 1) * 512], bank(bk)),
                          reads=[bbank[bk]], writes=[bdst])
        for which, src_tm, bsrc in (("k", k_tm, b_ktm), ("v", v_tm, b_vtm)):
            for og in range(2):
                em.group("tensor", [
                    (lambda tc, og=og, src_tm=src_tm: (lambda t: t.matmul(
                        bank(7)[0:1, :], ones128_bf[:, 0:1], src_tm[:, tc, og * 512:(og + 1) * 512],
                        start=(tc == 0), stop=(tc == NTCH - 1))))(tc)
                    for tc in range(NTCH)
                ], reads=[p.buf("ones128_bf"), bsrc], writes=[bbank[7]])
                em.op("vector", lambda v, og=og: v.tensor_copy(
                    rowf[0:1, og * 512:og * 512 + 512], bank(7)[0:1, :]),
                      reads=[bbank[7]], writes=[b_rowf])
            if which == "k":
                em.op("vector", lambda v: v.tensor_copy(rows_sb[0:1, 2048:3072], rowf[0:1, 0:1024]),
                      reads=[b_rowf], writes=[b_rows])
            else:
                em.op("vector", lambda v: v.scalar_tensor_tensor(
                    rows_sb[0:1, 3072:4096], rows_sb[0:1, 1024:2048], float(T),
                    rowf[0:1, 0:1024], ALU.mult, ALU.add),
                      reads=[b_rows, b_rowf], writes=[b_rows])
        for c1 in range(NCHUNK):
            for og in range(2):
                bk = og
                fns = [
                    (lambda tc, c1=c1, og=og, bk=bk: (lambda t: t.matmul(
                        bank(bk), k_tm[:, tc, c1 * 128:(c1 + 1) * 128],
                        v_tm[:, tc, og * 512:(og + 1) * 512],
                        start=(tc == 0), stop=False)))(tc)
                    for tc in range(NTCH)
                ]
                fns.append(lambda t, c1=c1, og=og, bk=bk: t.matmul(
                    bank(bk), rows_sb[0:1, c1 * 128:(c1 + 1) * 128],
                    rows_sb[0:1, 3072 + og * 512:3072 + og * 512 + 512],
                    start=False, stop=False))
                fns.append(lambda t, c1=c1, og=og, bk=bk: t.matmul(
                    bank(bk), rows_sb[0:1, 2048 + c1 * 128:2048 + c1 * 128 + 128],
                    rows_sb[0:1, 1024 + og * 512:1024 + og * 512 + 512],
                    start=False, stop=True))
                em.group("tensor", fns, reads=[b_ktm, b_vtm, b_rows], writes=[bbank[bk]])
                em.op("vector", lambda v, c1=c1, og=og, bk=bk: v.tensor_scalar_mul(
                    KV_sb[:, c1, og * 512:(og + 1) * 512], bank(bk), 1.0 / 32.0),
                      reads=[bbank[bk]], writes=[b_KV])
        em.full_barrier()
        kv_stack.close()
        q_stack = contextlib.ExitStack()
        q_fm = q_stack.enter_context(nc.sbuf_tensor("q_fm", [128, NCHUNK, T], BF16))
        b_qfm = p.buf("q_fm")
        wt2_stack = contextlib.ExitStack()
        wt2_sb = wt2_stack.enter_context(nc.sbuf_tensor("wt2_sb", [128, NCHUNK, 512], BF16))
        b_wt2 = p.buf("wt2_sb")
        for oc in range(NCHUNK):
            if oc % 4 == 0:
                em.op("gpsimd", lambda g, oc=oc: g.dma_start(
                    out=wt2_sb[:, :, :],
                    in_=d["qkv_wT"].rearrange("(c p) o -> p c o", p=128)[:, :, oc * 128:(oc + 4) * 128]),
                      writes=[b_wt2], dma=True)
            for tg in range(4):
                ts = slice(tg * 512, (tg + 1) * 512)
                bk = (oc * 4 + tg) % 8
                em.group("tensor", [
                    (lambda ck, oc=oc, ts=ts, bk=bk: (lambda t: t.matmul(
                        bank(bk), wt2_sb[:, ck, (oc % 4) * 128:(oc % 4 + 1) * 128], lnf[:, ck, ts],
                        start=(ck == 0), stop=(ck == NCHUNK - 1))))(ck)
                    for ck in range(NCHUNK)
                ], reads=[b_wt2] + blnf, writes=[bbank[bk]])
                em.op("vector", lambda v, oc=oc, ts=ts, bk=bk: v.tensor_scalar(
                    q_fm[:, oc, ts], bank(bk), 1.0, bqf[:, oc:oc + 1],
                    ALU.mult, ALU.add), reads=[bbank[bk], b_bqf], writes=[b_qfm])
        em.full_barrier()
        wt2_stack.close()
        o_slab = q_stack.enter_context(nc.sbuf_tensor("o_slab", [128, NCHUNK, 512], BF16))
        b_osl = p.buf("o_slab")
        pw_sb = q_stack.enter_context(nc.sbuf_tensor("pw_sb", [128, NCHUNK, D], BF16))
        b_pw = p.buf("pw_sb")
        pb_sb = q_stack.enter_context(nc.sbuf_tensor("pb_sb", [128, 8], F32))
        b_pb = p.buf("pb_sb")
        em.op("gpsimd", lambda g: g.dma_start(
            out=pw_sb[:, :, :], in_=d["proj_wT"].rearrange("(c p) o -> p c o", p=128)),
              writes=[b_pw], dma=True)
        em.op("sync", lambda s: s.dma_start(out=pb_sb[:, :], in_=d["proj_b_pc"][:, :]),
              writes=[b_pb], dma=True)
        for tg in range(4):
            ts = slice(tg * 512, (tg + 1) * 512)
            for c2 in range(NCHUNK):
                em.group("tensor", [
                    (lambda c1, c2=c2, ts=ts: (lambda t: t.matmul(
                        bank(c2), KV_sb[:, c1, c2 * 128:(c2 + 1) * 128], q_fm[:, c1, ts],
                        start=(c1 == 0), stop=(c1 == NCHUNK - 1))))(c1)
                    for c1 in range(NCHUNK)
                ], reads=[b_KV, b_qfm], writes=[bbank[c2]])
                em.op("vector", lambda v, c2=c2: v.tensor_copy(
                    o_slab[:, c2, :], bank(c2)), reads=[bbank[c2]], writes=[b_osl])
            for o2 in range(NCHUNK):
                em.group("tensor", [
                    (lambda c2, o2=o2: (lambda t: t.matmul(
                        bank(o2), pw_sb[:, c2, o2 * 128:(o2 + 1) * 128], o_slab[:, c2, :],
                        start=(c2 == 0), stop=(c2 == NCHUNK - 1))))(c2)
                    for c2 in range(NCHUNK)
                ], reads=[b_pw, b_osl], writes=[bbank[o2]])
                em.op("vector", lambda v, o2=o2, ts=ts: v.scalar_tensor_tensor(
                    x_sb[:, o2, ts], bank(o2), pb_sb[:, o2:o2 + 1], x_sb[:, o2, ts],
                    ALU.add, ALU.add), reads=[bbank[o2], b_pb] + bx, writes=bx)
        em.full_barrier()
        q_stack.close()
        if debug_taps:
            tap("x3", (128, NCHUNK, T))
            em.op("scalar", lambda s: s.dma_start(out=taps["x3"][:, :, :], in_=x_sb[:, :, :]),
                  reads=bx, dma=True)

        phase_open()
        ln_feature_major(p, sbp, bx, x_sb, lnf, blnf, stats, consts)
        rt_sb = sbp("rt_sb", (128, NCHUNK, 8), BF16)
        b_rt = p.buf("rt_sb")
        em.op("gpsimd", lambda g: g.dma_start(
            out=rt_sb[:, :, :], in_=d["router_T"].rearrange("(c p) e -> p c e", p=128)),
              writes=[b_rt], dma=True)
        mask_tm = sbp("mask_tm", (128, NTCH, 8))
        b_mask = p.buf("mask_tm")
        rmax = sbp("rmax", (128, 1)); b_rmax = p.buf("rmax")
        for tc in range(NTCH):
            em.group("tensor", [
                (lambda ck, tc=tc: (lambda t: t.matmul(
                    bank(0)[:, 0:8], lnf[:, ck, tc * 128:(tc + 1) * 128], rt_sb[:, ck, :],
                    start=(ck == 0), stop=(ck == NCHUNK - 1))))(ck)
                for ck in range(NCHUNK)
            ], reads=blnf + [b_rt], writes=[bbank[0]])
            em.op("vector", lambda v, tc=tc: v.reduce_max(rmax[:, :], bank(0)[:, 0:8], AX.X),
                  reads=[bbank[0]], writes=[b_rmax])
            em.op("vector", lambda v, tc=tc: v.tensor_single_scalar(
                mask_tm[:, tc, :], bank(0)[:, 0:8], rmax[:, :], ALU.is_ge),
                  reads=[bbank[0], b_rmax], writes=[b_mask])
        maskT = sbp("maskT", (8, T), BF16)
        b_maskT = p.buf("maskT")
        for tc in range(NTCH):
            em.op("tensor", lambda t, tc=tc: t.transpose(
                bank(1)[0:8, 0:128], mask_tm[:, tc, :], iden_sb[:, :]),
                  reads=[b_mask, p.buf("iden_sb")], writes=[bbank[1]])
            em.op("vector", lambda v, tc=tc: v.tensor_copy(
                maskT[:, tc * 128:(tc + 1) * 128], bank(1)[0:8, 0:128]),
                  reads=[bbank[1]], writes=[b_maskT])
        p8_sb = sbp("p8_sb", (8, 4096), BF16)
        b_p8 = p.buf("p8_sb")
        em.op("gpsimd", lambda g: g.dma_start(out=p8_sb[:, :], in_=d["p8pat"][:, :]),
              writes=[b_p8], dma=True)
        h_sb = sbp("h_sb", (128, 32, 1024), BF16)
        b_h = p.buf("h_sb")
        lscr = sbp("lscr", (128, 512))
        b_lscr = p.buf("lscr")
        mexp = sbp("mexp", (128, 512), BF16)
        b_mexp = p.buf("mexp")
        wblk = [sbp("wblk0", (128, NCHUNK, 128), BF16), sbp("wblk1", (128, NCHUNK, 128), BF16)]
        b_wblk = [p.buf("wblk0"), p.buf("wblk1")]
        pblk = [sbp("pblk0", (128, D), BF16), sbp("pblk1", (128, D), BF16)]
        b_pblk = [p.buf("pblk0"), p.buf("pblk1")]
        for half in range(2):
            hts = slice(half * 1024, (half + 1) * 1024)
            for ehc in range(32):
                wb = wblk[ehc % 2]; bwb = b_wblk[ehc % 2]
                em.op("gpsimd", lambda g, ehc=ehc, wb=wb: g.dma_start(
                    out=wb[:, :, :],
                    in_=d["w_fcT"].rearrange("(c p) e -> p c e", p=128)[:, :, ehc * 128:(ehc + 1) * 128]),
                      writes=[bwb], dma=True)
                for tg in range(2):
                    ts = slice(half * 1024 + tg * 512, half * 1024 + tg * 512 + 512)
                    bk = (ehc * 2 + tg) % 6
                    em.group("tensor", [
                        (lambda ck, ehc=ehc, ts=ts, bk=bk, wb=wb: (lambda t: t.matmul(
                            bank(bk), wb[:, ck, :], lnf[:, ck, ts],
                            start=(ck == 0), stop=(ck == NCHUNK - 1))))(ck)
                        for ck in range(NCHUNK)
                    ], reads=[bwb] + blnf, writes=[bbank[bk]])
                    em.op("vector", lambda v, bk=bk: v.tensor_scalar_mul(
                        lscr[:, :], bank(bk), 0.5), reads=[bbank[bk]], writes=[b_lscr])
                    em.op("vector", lambda v, ehc=ehc, tg=tg, bk=bk: v.tensor_max(
                        h_sb[:, ehc, tg * 512:(tg + 1) * 512], bank(bk), lscr[:, :]),
                          reads=[bbank[bk], b_lscr], writes=[b_h])
                    em.op("tensor", lambda t, ehc=ehc, ts=ts: t.matmul(
                        bank(6)[:, :], p8_sb[:, ehc * 128:(ehc + 1) * 128], maskT[:, ts],
                        start=True, stop=True), reads=[b_p8, b_maskT], writes=[bbank[6]])
                    em.op("vector", lambda v, ehc=ehc, tg=tg: v.tensor_mul(
                        h_sb[:, ehc, tg * 512:(tg + 1) * 512],
                        h_sb[:, ehc, tg * 512:(tg + 1) * 512], bank(6)[:, :]),
                          reads=[b_h, bbank[6]], writes=[b_h])
                    em.op("vector", lambda v, ehc=ehc, tg=tg: v.tensor_mul(
                        h_sb[:, ehc, tg * 512:(tg + 1) * 512],
                        h_sb[:, ehc, tg * 512:(tg + 1) * 512],
                        h_sb[:, ehc, tg * 512:(tg + 1) * 512]),
                          reads=[b_h], writes=[b_h])
            for st in range(2):
                sts = slice(st * 512, (st + 1) * 512)
                for ehc in range(32):
                    pb = pblk[ehc % 2]; bpb = b_pblk[ehc % 2]
                    em.op("gpsimd", lambda g, ehc=ehc, pb=pb: g.dma_start(
                        out=pb[:, :], in_=d["w_projT"][ehc * 128:(ehc + 1) * 128, :]),
                          writes=[bpb], dma=True)
                    for dc in range(NCHUNK):
                        em.op("tensor", lambda t, ehc=ehc, dc=dc, pb=pb, sts=sts: t.matmul(
                            bank(dc), pb[:, dc * 128:(dc + 1) * 128], h_sb[:, ehc, sts],
                            start=(ehc == 0), stop=(ehc == 31)),
                              reads=[bpb, b_h], writes=[bbank[dc]])
                for dc in range(NCHUNK):
                    em.op("vector", lambda v, dc=dc, half=half, st=st: v.tensor_add(
                        x_sb[:, dc, half * 1024 + st * 512: half * 1024 + st * 512 + 512],
                        x_sb[:, dc, half * 1024 + st * 512: half * 1024 + st * 512 + 512],
                        bank(dc)), reads=[bbank[dc]] + bx, writes=bx)

        phase_open()
        em.op("scalar", lambda s: s.dma_start(out=d["y"].rearrange("(c p) t -> p c t", p=128), in_=x_sb[:, :, :]),
              reads=bx, dma=True)
        em.finish("sync")

    stack.close()
    em.close()
    return nc


_RETURN_RES = [False]


def kernel(**inputs):
    g = host_prepare(inputs)
    nc = build_program()
    in_maps = []
    for b in range(B):
        im = {"xT": g["per_core_xT"][b]}
        for k in ("ones128", "iden", "mpat", "iota64", "w_q", "keysT", "bn_gamma", "bn_beta",
                  "dwu", "l1_norm_w", "l1_norm_b", "qkv_wT", "bq_pc", "bkv_row",
                  "proj_wT", "proj_b_pc", "router_T", "w_fcT", "w_projT", "p8pat"):
            im[k] = g[k]
        in_maps.append(im)
    res = run_bass_kernel_spmd(nc, in_maps, core_ids=list(range(B)))
    outs = [r["y"] for r in res.results]
    full = np.stack([o.reshape(D, T).T for o in outs])
    if _RETURN_RES[0]:
        return full, res
    return full



